# version 03 (final)
# baseline (speedup 1.0000x reference)
import contextlib
import numpy as np
import concourse.bass as bass
import concourse.mybir as mybir
from concourse.bass_utils import run_bass_kernel_spmd

F32 = mybir.dt.float32
I16 = mybir.dt.int16
BF16 = mybir.dt.bfloat16
AF = mybir.ActivationFunctionType
OP = mybir.AluOpType
AX = mybir.AxisListType

NCORE = 8
DM = 1024
L = 2048
DIN = 2048
DS = 16
DCONV = 4
DTR = 64
DFF = 4096
NEXP = 8
CAP = 2560
TCH = 512
NCH = L // TCH
DG = DM // 128
NG = DIN // 128
FG = DFF // 128
ETT = CAP // TCH
EPS_LN = 1e-5
EPS_RMS = 1.1920929e-07

ENGINES = ("sync", "scalar", "vector", "tensor", "gpsimd")


class Sem:
    __slots__ = ("h", "n")

    def __init__(self, handle):
        self.h = handle
        self.n = 0


class Prog:
    def __init__(self):
        self.q = {e: [] for e in ENGINES}

    def emit(self, eng, fn):
        self.q[eng].append(fn)

    def wait(self, eng, sem, val=None):
        v = sem.n if val is None else val
        if v > 0:
            self.emit(eng, lambda E, h=sem.h, v=v: E.wait_ge(h, v))

    def dma(self, eng, sem, out, in_, slow=False):
        sem.n += 16
        self.emit(eng, lambda E, o=out, i=in_, h=sem.h, sl=slow:
                  E.dma_start(out=o, in_=i, allow_slow_non_contiguous=sl)
                  .then_inc(h, 16))
        return sem.n

    def op(self, eng, fn, sem=None):
        if sem is None:
            self.emit(eng, fn)
            return None
        sem.n += 1
        self.emit(eng, lambda E, f=fn, h=sem.h: f(E).then_inc(h, 1))
        return sem.n


class Ctx:
    pass


def barrier(c, p):
    sems = [c.ql, c.qs, c.qg, c.st, c.sv, c.sa, c.sg, c.cc]
    for eng in ENGINES:
        for s in sems:
            p.wait(eng, s)


def build_program(stage=5, debug_taps=(), p45_cut=99):
    nc = bass.Bass("TRN2", target_bir_lowering=False, detect_race_conditions=False)
    c = Ctx()
    c.nc = nc
    c.stage = stage
    c.p45_cut = p45_cut
    c.taps = {}
    c.debug_taps = set(debug_taps)
    p = Prog()

    def din(name, shape, dtype=F32):
        return nc.dram_tensor(name, shape, dtype, kind="ExternalInput").ap()

    c.x_fm = din("x_fm", [DM, L])
    c.w_inproj = din("w_inproj", [32, 128, DG * 128])
    c.w_xproj = din("w_xproj", [128, NG * 96])
    c.w_dtproj = din("w_dtproj", [DTR, DIN])
    c.dtb_fm = din("dtb_fm", [128, NG])
    c.convw_fm = din("convw_fm", [128, NG * DCONV])
    c.convb_fm = din("convb_fm", [128, NG])
    c.alog_fm = din("alog_fm", [128, NG * DS])
    c.d_fm = din("d_fm", [128, NG])
    c.w_oproj = din("w_oproj", [DG, 128, NG * 128])
    c.w_mlp1 = din("w_mlp1", [FG, 128, DG * 128])
    c.w_mlp2 = din("w_mlp2", [DG, 128, FG * 128])
    c.w_r = din("w_r", [128, DG * NEXP])
    c.w_e1 = din("w_e1", [FG, 128, DG * 128])
    c.w_e2 = din("w_e2", [DG, 128, FG * 128])
    c.ident = din("ident", [128, 128])
    c.ecol = din("ecol", [128, 1])
    c.ohprev = din("ohprev", [1, NEXP])
    c.ohme = din("ohme", [128, 1])
    c.tloc = din("tloc", [1, L])
    c.offs = din("offs", [1, 8])
    c.jcol = din("jcol", [128, 8])
    c.tgid1 = din("tgid1", [1, L])

    c.y_out = nc.dram_tensor("y", [L, DM], F32, kind="ExternalOutput").ap()

    def tap(name, shape):
        if name in c.debug_taps:
            if name in c.taps:
                return c.taps[name]
            t = nc.dram_tensor("tap_" + name, shape, F32, kind="ExternalOutput").ap()
            c.taps[name] = t
            return t
        return None
    c.tap = tap

    def dsc(name, shape, dtype=F32):
        return nc.dram_tensor(name, shape, dtype, kind="Internal").ap()

    c.gy_fm = dsc("gy_fm", [DIN, L])
    c.rms2_fm = dsc("rms2_fm", [DM, L])
    c.xu_fm = dsc("xu_fm", [DM, L])
    c.h_fm = dsc("h_fm", [DFF, L])
    c.he_fm = dsc("he_fm", [DFF, CAP])
    c.x2tm = dsc("x2tm", [L, DM])
    c.ag1_in = dsc("ag1_in", [L, DM])
    c.ag1_out = dsc("ag1_out", [NCORE * L, DM])
    c.rt_in = dsc("rt_in", [2, L])
    c.rt_out = dsc("rt_out", [2 * NCORE, L])
    c.tab_in = dsc("tab_in", [1, NEXP * CAP + L])
    c.tab_red = dsc("tab_red", [1, NEXP * CAP + L])
    c.sidx_d = dsc("sidx_d", [1, L], mybir.dt.int32)
    c.gidx_d = dsc("gidx_d", [1, CAP], mybir.dt.int32)
    c.flat_d = dsc("flat_d", [1, L], mybir.dt.int32)
    c.fac_d = dsc("fac_d", [2, L])
    c.nrow_d = dsc("nrow_d", [1, 8])
    c.ag2_in = dsc("ag2_in", [CAP, DM])
    c.ag2_out = dsc("ag2_out", [NCORE * CAP, DM])

    c.stack = contextlib.ExitStack()

    def sb(name, shape, dtype=F32):
        return c.stack.enter_context(nc.sbuf_tensor(name, shape, dtype))

    c.ARX = sb("ARX", [128, 8448])
    c.ARY = sb("ARY", [128, 8192])
    c.ARZ = sb("ARZ", [128, 8192])
    c.ARW = sb("ARW", [128, 6144])
    c.SMA = sb("SMA", [128, 4096])
    c.WDBL = sb("WDBL", [128, 2048])
    c.CONST = sb("CONST", [128, 1024])
    c.STAT = sb("STAT", [128, 3072])
    c.XPW = sb("XPW", [128, NG * 96])
    c.DTW = sb("DTW", [DTR, DIN])
    c.ROUTE = sb("ROUTE", [16, 1024])
    c.HST = sb("HST", [128, NG * DS])
    c.ONES1 = sb("ONES1", [1, 128])
    c.I32B = sb("I32B", [1, 2560], mybir.dt.int32)
    c.I32T = sb("I32T", [128, 40], mybir.dt.int32)

    def psb(name, shape):
        return c.stack.enter_context(nc.psum_tensor(name, shape, F32))

    c.PS = [psb(f"ps{i}", [128, 512]) for i in range(6)]
    c.PS_R1 = psb("ps_r1", [1, 512])
    c.PS_R2 = psb("ps_r2", [1, 512])

    c.A = c.CONST[:, 0:256]
    c.Dv = c.CONST[:, 256:272]
    c.CW = c.CONST[:, 272:336]
    c.CB = c.CONST[:, 336:352]
    c.DTB = c.CONST[:, 352:368]
    c.IDENT = c.CONST[:, 384:512]
    c.ONESC = c.CONST[:, 640:641]
    c.EPSL = c.CONST[0:1, 641:642]
    c.EPSR = c.CONST[0:1, 642:643]
    c.WRS = c.CONST[:, 704:768]
    c.ECOL = c.CONST[:, 768:769]
    c.OHME = c.CONST[:, 769:770]
    c.OHPREV = c.CONST[:, 770:778]
    c.OFFS = c.CONST[0:1, 778:786]
    c.JC8 = c.CONST[:, 786:794]

    for nm in ("ql", "qs", "qg", "st", "sv", "sa", "sg", "cc"):
        setattr(c, nm, Sem(c.stack.enter_context(nc.semaphore(nm))))

    plan_p0(c, p)
    plan_p1(c, p)
    if stage >= 2:
        plan_p2(c, p)
    if stage >= 3:
        plan_p3(c, p)
    if stage >= 4:
        plan_p45(c, p)
    if stage >= 5:
        plan_p6(c, p)
        plan_p7(c, p)
    plan_finalize(c, p)

    with nc.Block() as block:
        for eng in ENGINES:
            fns = p.q[eng]
            if not fns:
                continue

            def body(E, fns=fns):
                for fn in fns:
                    fn(E)
            getattr(block, eng)(body)
    c.stack.close()
    return nc, c


def plan_p0(c, p):
    ql, sa, sv = c.ql, c.sa, c.sv
    p.dma("sync", ql, c.A, c.alog_fm[:, :])
    p.dma("sync", ql, c.Dv, c.d_fm[:, :])
    p.dma("sync", ql, c.CW, c.convw_fm[:, :])
    p.dma("sync", ql, c.CB, c.convb_fm[:, :])
    p.dma("sync", ql, c.DTB, c.dtb_fm[:, :])
    p.dma("sync", ql, c.IDENT, c.ident[:, :])
    p.dma("sync", ql, c.XPW[:, :], c.w_xproj[:, :])
    p.dma("sync", ql, c.DTW[:, :], c.w_dtproj[:, :])
    p.dma("sync", ql, c.WRS, c.w_r[:, :])
    p.dma("sync", ql, c.ECOL, c.ecol[:, :])
    p.dma("sync", ql, c.OHME, c.ohme[:, :])
    p.dma("sync", ql, c.OFFS, c.offs[:, :])
    p.dma("sync", ql, c.JC8, c.jcol[:, :])
    p.dma("sync", ql, c.OHPREV[0:1, :], c.ohprev[:, :])
    p.wait("scalar", ql)
    p.op("scalar", lambda E: E.activation(c.A, c.A, AF.Exp), sa)
    p.wait("vector", sa)
    p.op("vector", lambda E: E.tensor_scalar_mul(c.A, c.A, -1.0))
    p.op("vector", lambda E: E.memset(c.ONESC, 1.0))
    p.op("vector", lambda E: E.memset(c.EPSL, EPS_LN))
    p.op("vector", lambda E: E.memset(c.EPSR, EPS_RMS))
    p.op("vector", lambda E: E.memset(c.HST[:, :], 0.0))
    p.op("vector", lambda E: E.memset(c.ONES1[:, :], 1.0), sv)
    c.p0_v = sv.n


def plan_feat_sums(c, p, xviews, T, ready):
    st, sa = c.st, c.sa
    n = len(xviews)
    p.wait("tensor", c.sv, c.sv.n)
    for s_, v_ in ready:
        p.wait("scalar", s_, v_)
        p.wait("tensor", s_, v_)
    for g in range(n):
        p.op("scalar",
             lambda E, g=g: E.activation(c.SMA[:, g * T:(g + 1) * T], xviews[g],
                                         AF.Square),
             sa if g == n - 1 else None)
    sqv = sa.n
    for g in range(n):
        p.op("tensor",
             lambda E, g=g, s=(g == 0), e=(g == n - 1):
                 E.matmul(c.PS_R1[0:1, 0:T], c.ONESC, xviews[g], start=s, stop=e))
    p.wait("tensor", sa, sqv)
    for g in range(n):
        p.op("tensor",
             lambda E, g=g, s=(g == 0), e=(g == n - 1):
                 E.matmul(c.PS_R2[0:1, 0:T], c.ONESC,
                          c.SMA[:, g * T:(g + 1) * T], start=s, stop=e),
             st if g == n - 1 else None)
    return st.n


def plan_bcast_rows(c, p, rows, wait_sem, wait_val):
    st, sa = c.st, c.sa
    p.wait("tensor", wait_sem, wait_val)
    for i, (src, dst, T) in enumerate(rows):
        bank = c.PS[4] if i % 2 == 0 else c.PS[5]
        p.op("tensor", lambda E, s=src, b=bank, T=T:
             E.matmul(b[:, 0:T], c.ONES1[:, :], s, start=True, stop=True), st)
        p.wait("scalar", st)
        p.op("scalar", lambda E, d=dst, b=bank, T=T: E.copy(d, b[:, 0:T]), sa)
    return sa.n


def plan_gemm_stream(c, p, wtiles, nj, nk, rhs_views, T, evac, ready,
                     banks=(0, 1), tile_free=None):
    st, sa = c.st, c.sa
    tile_free = tile_free or nk * 128
    slot = [c.WDBL[:, 0:1024], c.WDBL[:, 1024:2048]]
    mm_done = [0] * nj
    evac_done = [0] * nj
    for s_, v_ in ready:
        p.wait("sync", s_, v_)
        p.wait("tensor", s_, v_)
        p.wait("scalar", s_, v_)
    for j in range(nj):
        if j >= 2:
            p.wait("sync", st, mm_done[j - 2])
        p.dma("sync", c.ql, slot[j % 2][:, 0:tile_free], wtiles[j])
        ld = c.ql.n
        p.wait("tensor", c.ql, ld)
        bank = c.PS[banks[0]] if j % 2 == 0 else c.PS[banks[1]]
        if j >= 2:
            p.wait("tensor", sa, evac_done[j - 2])
        for k in range(nk):
            p.op("tensor", lambda E, k=k, b=bank, sl=slot[j % 2],
                 s=(k == 0), e=(k == nk - 1), T=T:
                 E.matmul(b[:, 0:T], sl[:, k * 128:(k + 1) * 128],
                          rhs_views[k], start=s, stop=e),
                 st if k == nk - 1 else None)
        mm_done[j] = st.n
        p.wait("scalar", st, mm_done[j])
        evac(p, j, bank[:, 0:T])
        evac_done[j] = sa.n
    return evac_done


def ev_copy(c, outs):
    def f(p, j, bank):
        p.op("scalar", lambda E, j=j, b=bank: E.copy(outs[j], b), c.sa)
    return f


def ev_act(c, outs, func):
    def f(p, j, bank):
        p.op("scalar", lambda E, j=j, b=bank: E.activation(outs[j], b, func), c.sa)
    return f


def plan_p1(c, p):
    ql, qs, st, sv, sa, sg = c.ql, c.qs, c.st, c.sv, c.sa, c.sg

    XBUF = [c.ARY[:, g * TCH:(g + 1) * TCH] for g in range(DG)]
    RMS1 = [c.ARW[:, g * TCH:(g + 1) * TCH] for g in range(DG)]
    SC_BC = c.ARW[:, 4096:4608]
    MSC_BC = c.ARW[:, 4608:5120]
    CMB = c.ARW[:, 4096:4608]
    XPg = [c.ARX[:, g * 516 + 3:g * 516 + 3 + TCH] for g in range(NG)]
    XPfull = [c.ARX[:, g * 516:g * 516 + 3 + TCH] for g in range(NG)]
    HALO = [c.ARX[:, g * 516:g * 516 + 3] for g in range(NG)]
    XPtail = [c.ARX[:, g * 516 + 512:g * 516 + 515] for g in range(NG)]
    XC = [c.ARZ[:, g * TCH:(g + 1) * TCH] for g in range(NG)]
    DT = [c.ARX[:, g * TCH:(g + 1) * TCH] for g in range(NG)]
    Z = DT
    DTXC = [c.ARY[:, g * TCH:(g + 1) * TCH] for g in range(NG)]

    DA = [c.SMA[:, 0:512], c.SMA[:, 512:1024]]
    BT = [c.SMA[:, 1024:1536], c.SMA[:, 1536:2048]]
    HT = [c.SMA[:, 2048:2560], c.SMA[:, 2560:3072]]
    HC = [c.SMA[:, 3072:3584], c.SMA[:, 3584:4096]]
    CONVS = [c.SMA[:, 0:512], c.SMA[:, 512:1024]]

    SRow = lambda i: c.STAT[0:1, i * 512:(i + 1) * 512]
    MEANR, VARR, SCR, TMPR = SRow(0), SRow(1), SRow(2), SRow(3)
    XD = c.STAT[0:64, 2048:2560]
    HALOSV = [c.STAT[:, 2560 + g * 3:2560 + g * 3 + 3] for g in range(NG)]
    BM = c.ROUTE[0:16, 0:512]
    CM = c.ROUTE[0:16, 512:1024]

    PS_A, PS_B, PS_XD, PS_BCB = c.PS[0], c.PS[1], c.PS[2], c.PS[3]

    tap_rms1 = c.tap("rms1", [NCH, 128, DG * TCH])
    tap_xc = c.tap("xc", [NCH, 128, NG * TCH])
    tap_dt = c.tap("dt", [NCH, 128, NG * TCH])
    tap_bm = c.tap("bm", [NCH, 16, TCH])
    tap_cm = c.tap("cm", [NCH, 16, TCH])
    tap_y = c.tap("y", [NCH, 128, NG * TCH])
    tap_gy = c.tap("gy", [NCH, 128, NG * TCH])
    tap_z = c.tap("z", [NCH, 128, NG * TCH])

    p.wait("sync", sv, c.p0_v)

    for ch in range(NCH):
        t0 = ch * TCH
        barrier(c, p)

        if ch == 0:
            for g in range(NG):
                p.op("scalar", lambda E, g=g: E.memzero(HALO[g]),
                     sa if g == NG - 1 else None)
        else:
            for g in range(NG):
                p.op("scalar", lambda E, g=g: E.copy(HALO[g], HALOSV[g]),
                     sa if g == NG - 1 else None)
        halo_v = sa.n

        for g in range(DG):
            p.dma("sync", ql, XBUF[g], c.x_fm[g * 128:(g + 1) * 128, t0:t0 + TCH])
        x_ld = ql.n

        sums_v = plan_feat_sums(c, p, XBUF, TCH, [(ql, x_ld)])
        p.wait("vector", st, sums_v)
        p.op("vector", lambda E: E.tensor_scalar_mul(MEANR, c.PS_R1[0:1, 0:TCH], 1.0 / DM))
        p.op("vector", lambda E: E.tensor_scalar_mul(VARR, c.PS_R2[0:1, 0:TCH], 1.0 / DM))
        p.op("vector", lambda E: E.tensor_mul(TMPR, MEANR, MEANR))
        p.op("vector", lambda E: E.tensor_sub(VARR, VARR, TMPR), sv)
        p.wait("scalar", sv)
        p.op("scalar", lambda E: E.activation(SCR, VARR, AF.Sqrt, bias=c.EPSL), sa)
        p.wait("vector", sa)
        p.op("vector", lambda E: E.reciprocal(SCR, SCR))
        p.op("vector", lambda E: E.tensor_mul(TMPR, SCR, SCR))
        p.op("vector", lambda E: E.tensor_mul(TMPR, TMPR, VARR), sv)
        p.wait("scalar", sv)
        p.op("scalar", lambda E: E.activation(TMPR, TMPR, AF.Sqrt, bias=c.EPSR), sa)
        p.wait("vector", sa)
        p.op("vector", lambda E: E.reciprocal(TMPR, TMPR))
        p.op("vector", lambda E: E.tensor_mul(SCR, SCR, TMPR))
        p.op("vector", lambda E: E.tensor_mul(TMPR, SCR, MEANR), sv)
        bc_v = plan_bcast_rows(c, p, [(SCR, SC_BC, TCH), (TMPR, MSC_BC, TCH)],
                               sv, sv.n)
        p.wait("vector", sa, bc_v)
        for g in range(DG):
            p.op("vector", lambda E, g=g: E.tensor_mul(RMS1[g], XBUF[g], SC_BC))
            p.op("vector", lambda E, g=g: E.tensor_sub(RMS1[g], RMS1[g], MSC_BC),
                 sv if g == DG - 1 else None)
        rms1_v = sv.n

        if tap_rms1 is not None:
            p.wait("sync", sv, rms1_v)
            p.dma("sync", qs, tap_rms1[ch, :, :], c.ARW[:, 0:DG * TCH])

        evac_xp = plan_gemm_stream(
            c, p, [c.w_inproj[j, :, :] for j in range(NG)], NG, DG, RMS1, TCH,
            ev_copy(c, XPg), ready=[(sv, rms1_v), (sa, halo_v)])

        silu_done = [0] * NG
        for g in range(NG):
            cs = CONVS[g % 2]
            p.wait("vector", sa, evac_xp[g])
            if g >= 2:
                p.wait("vector", sa, silu_done[g - 2])
            p.op("vector", lambda E, cs=cs, g=g:
                 E.tensor_scalar(cs, XPfull[g][:, 0:TCH],
                                 c.CW[:, g * 4:g * 4 + 1], c.CB[:, g:g + 1],
                                 OP.mult, OP.add))
            for k in range(1, 4):
                p.op("vector", lambda E, cs=cs, g=g, k=k:
                     E.scalar_tensor_tensor(cs, XPfull[g][:, k:k + TCH],
                                            c.CW[:, g * 4 + k:g * 4 + k + 1], cs,
                                            OP.mult, OP.add),
                     sv if k == 3 else None)
            p.wait("scalar", sv)
            p.op("scalar", lambda E, cs=cs, g=g: E.activation(XC[g], cs, AF.Silu), sa)
            silu_done[g] = sa.n
        silu_v = sa.n

        for g in range(NG):
            p.op("scalar", lambda E, g=g: E.copy(HALOSV[g], XPtail[g]),
                 sa if g == NG - 1 else None)
        tails_v = sa.n

        if tap_xc is not None:
            p.wait("sync", sa, silu_v)
            p.dma("sync", qs, tap_xc[ch, :, :], c.ARZ[:, 0:NG * TCH])

        p.wait("tensor", sa, silu_v)
        for k in range(NG):
            p.op("tensor", lambda E, k=k, s=(k == 0), e=(k == NG - 1):
                 E.matmul(PS_XD[0:64, 0:TCH], c.XPW[:, k * 96:k * 96 + 64],
                          XC[k], start=s, stop=e))
        for k in range(NG):
            p.op("tensor", lambda E, k=k, s=(k == 0), e=(k == NG - 1):
                 E.matmul(c.PS[4][0:16, 0:TCH], c.XPW[:, k * 96 + 64:k * 96 + 80],
                          XC[k], start=s, stop=e))
        for k in range(NG):
            p.op("tensor", lambda E, k=k, s=(k == 0), e=(k == NG - 1):
                 E.matmul(c.PS[5][0:16, 0:TCH], c.XPW[:, k * 96 + 80:k * 96 + 96],
                          XC[k], start=s, stop=e),
                 st if k == NG - 1 else None)
        xdbl_t = st.n
        p.wait("scalar", st, xdbl_t)
        p.op("scalar", lambda E: E.copy(XD, PS_XD[0:64, 0:TCH]))
        p.op("scalar", lambda E: E.copy(BM, c.PS[4][0:16, 0:TCH]))
        p.op("scalar", lambda E: E.copy(CM, c.PS[5][0:16, 0:TCH]), sa)
        xd_v = sa.n

        if tap_bm is not None:
            p.wait("sync", sa, xd_v)
            p.dma("sync", qs, tap_bm[ch, :, :], BM)
            p.dma("sync", qs, tap_cm[ch, :, :], CM)

        p.wait("tensor", sa, xd_v)
        p.wait("scalar", sa, tails_v)
        dt_evac = [0] * NG
        for dj in range(NG):
            bank = PS_A if dj % 2 == 0 else PS_B
            if dj >= 2:
                p.wait("tensor", sa, dt_evac[dj - 2])
            p.op("tensor", lambda E, dj=dj, b=bank:
                 E.matmul(b[:, 0:TCH], c.DTW[:, dj * 128:(dj + 1) * 128],
                          XD[:, 0:TCH], start=True, stop=True), st)
            mmv = st.n
            p.wait("scalar", st, mmv)
            p.op("scalar", lambda E, dj=dj, b=bank:
                 E.activation(DA[0], b[:, 0:TCH], AF.Exp, bias=c.DTB[:, dj:dj + 1]))
            p.op("scalar", lambda E, dj=dj:
                 E.activation(DT[dj], DA[0], AF.Ln, bias=c.ONESC), sa)
            dt_evac[dj] = sa.n
        dt_v = sa.n

        if tap_dt is not None:
            p.wait("sync", sa, dt_v)
            p.dma("sync", qs, tap_dt[ch, :, :], c.ARX[:, 0:NG * TCH])

        p.wait("vector", sa, dt_v)
        for g in range(NG):
            p.op("vector", lambda E, g=g: E.tensor_mul(DTXC[g], DT[g], XC[g]))
        for g in range(NG):
            p.op("vector", lambda E, g=g:
                 E.tensor_scalar_mul(XC[g], XC[g], c.Dv[:, g:g + 1]),
                 sv if g == NG - 1 else None)
        dtxc_v = sv.n

        da_free = [0, 0]
        ht_free_sa = [0, 0]
        ht_free_sg = [0, 0]
        hc_free = [0, 0]
        prev_scan_last = 0
        prev_cmb = 0
        p.wait("vector", sv, dtxc_v)
        for s in range(DS):
            if s > 0:
                p.wait("tensor", sv, prev_scan_last)
                p.wait("tensor", sa, prev_cmb)
            selap = bass.AP(c.IDENT.tensor, c.IDENT.offset + s, [[1024, 16], [0, 128]])
            p.op("tensor", lambda E, selap=selap:
                 E.matmul(PS_BCB[:, 0:TCH], selap, BM, start=True, stop=True))
            p.op("tensor", lambda E, selap=selap:
                 E.matmul(c.PS[4][:, 0:TCH], selap, CM, start=True, stop=True), st)
            bc_t = st.n
            p.wait("scalar", st, bc_t)
            p.op("scalar", lambda E: E.copy(CMB, c.PS[4][:, 0:TCH]), sa)
            cmb_v = sa.n
            prev_cmb = cmb_v
            p.wait("vector", st, bc_t)
            p.wait("gpsimd", sa, cmb_v)
            for g in range(NG):
                par = g % 2
                gs = g * DS + s
                p.wait("scalar", sv, da_free[par])
                p.op("scalar", lambda E, g=g, s=s, par=par:
                     E.activation(DA[par], DT[g], AF.Exp,
                                  scale=c.A[:, g * DS + s:g * DS + s + 1]), sa)
                da_v = sa.n
                p.wait("vector", sa, da_v)
                p.wait("vector", sa, ht_free_sa[par])
                p.wait("vector", sg, ht_free_sg[par])
                p.op("vector", lambda E, g=g, par=par:
                     E.tensor_mul(BT[par], DTXC[g], PS_BCB[:, 0:TCH]))
                p.op("vector", lambda E, par=par, gs=gs:
                     E.tensor_tensor_scan(HT[par], DA[par], BT[par],
                                          c.HST[:, gs:gs + 1], OP.mult, OP.add), sv)
                scan_v = sv.n
                da_free[par] = scan_v
                p.wait("scalar", sv, scan_v)
                p.op("scalar", lambda E, par=par, gs=gs:
                     E.copy(c.HST[:, gs:gs + 1], HT[par][:, TCH - 1:TCH]), sa)
                ht_free_sa[par] = sa.n
                p.wait("gpsimd", sv, scan_v)
                p.wait("gpsimd", sv, hc_free[par])
                p.op("gpsimd", lambda E, par=par:
                     E.tensor_mul(HC[par], HT[par], CMB), sg)
                ht_free_sg[par] = sg.n
                p.wait("vector", sg, sg.n)
                p.op("vector", lambda E, g=g, par=par:
                     E.tensor_add(XC[g], XC[g], HC[par]), sv)
                hc_free[par] = sv.n
                if g == NG - 1:
                    prev_scan_last = scan_v
        scan_done_v = sv.n

        if tap_y is not None:
            p.wait("sync", sv, scan_done_v)
            p.dma("sync", qs, tap_y[ch, :, :], c.ARZ[:, 0:NG * TCH])

        evac_z = plan_gemm_stream(
            c, p, [c.w_inproj[16 + j, :, :] for j in range(NG)], NG, DG, RMS1, TCH,
            ev_copy(c, Z), ready=[(sv, rms1_v), (sv, scan_done_v)])
        for g in range(NG):
            p.wait("scalar", sa, evac_z[g])
            p.op("scalar", lambda E, g=g: E.activation(Z[g], Z[g], AF.Silu),
                 sa if g == NG - 1 else None)
        sz_v = sa.n

        if tap_z is not None:
            p.wait("sync", sa, sz_v)
            p.dma("sync", qs, tap_z[ch, :, :], c.ARX[:, 0:NG * TCH])

        p.wait("vector", sa, sz_v)
        for g in range(NG):
            p.op("vector", lambda E, g=g: E.tensor_mul(XC[g], XC[g], Z[g]),
                 sv if g == NG - 1 else None)
        gy_v = sv.n

        if tap_gy is not None:
            p.wait("sync", sv, gy_v)
            p.dma("sync", qs, tap_gy[ch, :, :], c.ARZ[:, 0:NG * TCH])

        p.wait("sync", sv, gy_v)
        for g in range(NG):
            p.dma("sync", qs, c.gy_fm[g * 128:(g + 1) * 128, t0:t0 + TCH], XC[g])

    barrier(c, p)


def plan_p2(c, p):
    ql, qs, st, sv, sa = c.ql, c.qs, c.st, c.sv, c.sa
    barrier(c, p)

    def opj(dj):
        ar = c.ARX if dj < 4 else c.ARY
        return ar[:, (dj % 4) * 2048:(dj % 4) * 2048 + 2048]

    for dj in range(DG):
        p.dma("sync", ql, opj(dj), c.w_oproj[dj, :, :])
    opj_ld = ql.n

    GY = [c.ARZ[:, g * TCH:(g + 1) * TCH] for g in range(NG)]
    X = [c.ARW[:, g * TCH:(g + 1) * TCH] for g in range(DG)]
    U = [c.SMA[:, g * TCH:(g + 1) * TCH] for g in range(DG)]
    RMS2 = GY[:DG]
    SRow = lambda i: c.STAT[0:1, i * 512:(i + 1) * 512]
    MEANR, VARR, SCR, TMPR = SRow(0), SRow(1), SRow(2), SRow(3)
    SC_BC = c.ARW[:, 4096:4608]
    MSC_BC = c.ARW[:, 4608:5120]

    tap_u = c.tap("u", [NCH, 128, DG * TCH])
    tap_rms2 = c.tap("rms2", [NCH, 128, DG * TCH])
    tap_xu = c.tap("xu", [NCH, 128, DG * TCH])

    prev_mm = 0
    for tt in range(NCH):
        t0 = tt * TCH
        barrier(c, p)
        for g in range(NG):
            p.dma("sync", ql, GY[g], c.gy_fm[g * 128:(g + 1) * 128, t0:t0 + TCH])
        for g in range(DG):
            p.dma("sync", ql, X[g], c.x_fm[g * 128:(g + 1) * 128, t0:t0 + TCH])
        ld_v = ql.n

        sums_v = plan_feat_sums(c, p, X, TCH, [(ql, ld_v)])
        p.wait("vector", st, sums_v)
        p.op("vector", lambda E: E.tensor_scalar_mul(MEANR, c.PS_R1[0:1, 0:TCH], 1.0 / DM))
        p.op("vector", lambda E: E.tensor_scalar_mul(VARR, c.PS_R2[0:1, 0:TCH], 1.0 / DM))
        p.op("vector", lambda E: E.tensor_mul(TMPR, MEANR, MEANR))
        p.op("vector", lambda E: E.tensor_sub(VARR, VARR, TMPR), sv)
        p.wait("scalar", sv)
        p.op("scalar", lambda E: E.activation(SCR, VARR, AF.Sqrt, bias=c.EPSL), sa)
        p.wait("vector", sa)
        p.op("vector", lambda E: E.reciprocal(SCR, SCR))
        p.op("vector", lambda E: E.tensor_mul(TMPR, SCR, MEANR), sv)
        bc_v = plan_bcast_rows(c, p, [(SCR, SC_BC, TCH), (TMPR, MSC_BC, TCH)],
                               sv, sv.n)
        p.wait("vector", sa, bc_v)
        for g in range(DG):
            p.op("vector", lambda E, g=g: E.tensor_mul(U[g], X[g], SC_BC))
            p.op("vector", lambda E, g=g: E.tensor_sub(U[g], U[g], MSC_BC),
                 sv if g == DG - 1 else None)
        ln1_v = sv.n

        p.wait("tensor", ql, ld_v)
        p.wait("tensor", sv, ln1_v)
        evac_u = [0] * DG
        for dj in range(DG):
            bank = c.PS[0] if dj % 2 == 0 else c.PS[1]
            if dj >= 2:
                p.wait("tensor", sv, evac_u[dj - 2])
            for k in range(NG):
                p.op("tensor", lambda E, dj=dj, k=k, b=bank,
                     s=(k == 0), e=(k == NG - 1):
                     E.matmul(b[:, 0:TCH], opj(dj)[:, k * 128:(k + 1) * 128],
                              GY[k], start=s, stop=e),
                     st if k == NG - 1 else None)
            mmv = st.n
            p.wait("vector", st, mmv)
            p.op("vector", lambda E, dj=dj, b=bank:
                 E.tensor_add(U[dj], U[dj], b[:, 0:TCH]), sv)
            evac_u[dj] = sv.n
        u_v = sv.n

        if tap_u is not None:
            p.wait("sync", sv, u_v)
            p.dma("sync", qs, tap_u[tt, :, :], c.SMA[:, 0:DG * TCH])

        p.wait("vector", sv, u_v)
        for g in range(DG):
            p.op("vector", lambda E, g=g: E.tensor_add(X[g], X[g], U[g]),
                 sv if g == DG - 1 else None)
        xu_v = sv.n
        p.wait("sync", sv, xu_v)
        for g in range(DG):
            p.dma("sync", qs, c.xu_fm[g * 128:(g + 1) * 128, t0:t0 + TCH], X[g])
        if tap_xu is not None:
            p.dma("sync", qs, tap_xu[tt, :, :], c.ARW[:, 0:DG * TCH])

        p.wait("scalar", st, st.n)
        for g in range(DG):
            p.op("scalar", lambda E, g=g:
                 E.activation(c.ARZ[:, g * TCH:(g + 1) * TCH], U[g], AF.Square),
                 sa if g == DG - 1 else None)
        sq_v = sa.n
        p.wait("tensor", sa, sq_v)
        for g in range(DG):
            p.op("tensor", lambda E, g=g, s=(g == 0), e=(g == DG - 1):
                 E.matmul(c.PS_R2[0:1, 0:TCH], c.ONESC,
                          c.ARZ[:, g * TCH:(g + 1) * TCH], start=s, stop=e),
                 st if g == DG - 1 else None)
        ssq_t = st.n
        p.wait("vector", st, ssq_t)
        p.op("vector", lambda E: E.tensor_scalar_mul(VARR, c.PS_R2[0:1, 0:TCH], 1.0 / DM), sv)
        p.wait("scalar", sv)
        p.op("scalar", lambda E: E.activation(VARR, VARR, AF.Sqrt, bias=c.EPSR), sa)
        p.wait("vector", sa)
        p.op("vector", lambda E: E.reciprocal(VARR, VARR), sv)
        bc2_v = plan_bcast_rows(c, p, [(VARR, SC_BC, TCH)], sv, sv.n)
        p.wait("vector", sa, bc2_v)
        for g in range(DG):
            p.op("vector", lambda E, g=g:
                 E.tensor_mul(RMS2[g], U[g], SC_BC), sv if g == DG - 1 else None)
        rms2_v = sv.n
        p.wait("sync", sv, rms2_v)
        for g in range(DG):
            p.dma("sync", qs, c.rms2_fm[g * 128:(g + 1) * 128, t0:t0 + TCH], RMS2[g])
        if tap_rms2 is not None:
            p.dma("sync", qs, tap_rms2[tt, :, :], c.ARZ[:, 0:DG * TCH])
    barrier(c, p)


def plan_p3(c, p):
    ql, qs, st, sv, sa = c.ql, c.qs, c.st, c.sv, c.sa
    barrier(c, p)

    def r2blk(g):
        ar = c.ARX if g < 4 else c.ARY
        return ar[:, (g % 4) * 2048:(g % 4) * 2048 + 2048]

    for g in range(DG):
        p.dma("sync", ql, r2blk(g), c.rms2_fm[g * 128:(g + 1) * 128, :])
    r2_ld = ql.n
    p.wait("tensor", ql, r2_ld)

    HSTG = [c.SMA[:, 0:512], c.SMA[:, 512:1024]]
    slot = [c.WDBL[:, 0:1024], c.WDBL[:, 1024:2048]]
    mm_done = [0] * FG
    stg_free = [0, 0]
    evac_done = [0, 0]
    for fj in range(FG):
        if fj >= 2:
            p.wait("sync", st, mm_done[fj - 2])
        p.dma("sync", ql, slot[fj % 2], c.w_mlp1[fj, :, :])
        ld = ql.n
        p.wait("tensor", ql, ld)
        for tt in range(NCH):
            q = fj * NCH + tt
            bank = c.PS[q % 2]
            sidx = q % 2
            p.wait("tensor", sa, evac_done[q % 2])
            for k in range(DG):
                p.op("tensor", lambda E, k=k, b=bank, sl=slot[fj % 2], tt=tt,
                     s=(k == 0), e=(k == DG - 1):
                     E.matmul(b[:, 0:TCH], sl[:, k * 128:(k + 1) * 128],
                              r2blk(k)[:, tt * TCH:(tt + 1) * TCH],
                              start=s, stop=e),
                     st if k == DG - 1 else None)
            mmv = st.n
            p.wait("scalar", st, mmv)
            p.wait("scalar", c.qs, stg_free[sidx])
            p.op("scalar", lambda E, b=bank, sidx=sidx:
                 E.activation(HSTG[sidx], b[:, 0:TCH], AF.Silu), sa)
            hv = sa.n
            evac_done[q % 2] = hv
            p.wait("sync", sa, hv)
            p.dma("sync", qs, c.h_fm[fj * 128:(fj + 1) * 128,
                                     tt * TCH:(tt + 1) * TCH], HSTG[sidx])
            stg_free[sidx] = qs.n
        mm_done[fj] = st.n
    barrier(c, p)

    def htblk(k):
        ar = c.ARX if k < 16 else c.ARY
        return ar[:, (k % 16) * 512:(k % 16) * 512 + 512]

    XU = [c.ARW[:, g * TCH:(g + 1) * TCH] for g in range(DG)]
    LN2 = [c.SMA[:, g * TCH:(g + 1) * TCH] for g in range(DG)]
    W2S = [c.ARZ[:, 0:4096], c.ARZ[:, 4096:8192]]
    R1 = c.ROUTE[0:1, 0:512]
    R2 = c.ROUTE[0:1, 512:1024]
    R3 = c.STAT[0:1, 2048:2560]
    R4 = c.STAT[0:1, 2560:3072]
    SC_BC = c.ARW[:, 4096:4608]
    MSC_BC = c.ARW[:, 4608:5120]
    TMS1 = c.ARW[:, 5120:6144]
    TMS2 = c.STAT[:, 0:1024]
    LG = c.STAT[:, 1024:1032]
    MX = c.STAT[:, 1040:1041]
    MNEG = c.STAT[:, 1041:1042]
    SSUM = c.STAT[:, 1042:1043]
    PMX = c.STAT[:, 1043:1044]
    IDX8 = c.STAT[:, 1056:1064]
    IDX = c.STAT[:, 1056:1057]
    EXPO = c.STAT[:, 1064:1072]
    MX8B = c.STAT[:, 1072:1080]
    EQD = c.STAT[:, 1080:1088]
    MUL8 = c.STAT[:, 1088:1096]
    SPCV = c.STAT[:, 1200:1264]
    SCRAP = c.STAT[:, 1048:1056]

    tap_x2 = c.tap("x2", [NCH, 128, DG * TCH])
    tap_ln2 = c.tap("ln2", [NCH, 128, DG * TCH])

    w2_free = [0, 0]
    for tt in range(NCH):
        t0 = tt * TCH
        barrier(c, p)
        for k in range(FG):
            p.dma("sync", ql, htblk(k), c.h_fm[k * 128:(k + 1) * 128, t0:t0 + TCH])
        for g in range(DG):
            p.dma("sync", ql, XU[g], c.xu_fm[g * 128:(g + 1) * 128, t0:t0 + TCH])
        ld_v = ql.n
        p.wait("tensor", ql, ld_v)

        evac_x2 = [0] * DG
        for dj in range(DG):
            wslot = W2S[dj % 2]
            if dj >= 2:
                p.wait("sync", st, c._p3mm2[dj - 2] if hasattr(c, "_p3mm2") else 0)
            if dj == 0:
                c._p3mm2 = [0] * DG
            p.dma("sync", ql, wslot, c.w_mlp2[dj, :, :])
            wld = ql.n
            p.wait("tensor", ql, wld)
            bank = c.PS[0] if dj % 2 == 0 else c.PS[1]
            if dj >= 2:
                p.wait("tensor", sv, evac_x2[dj - 2])
            for k in range(FG):
                p.op("tensor", lambda E, k=k, b=bank, w=wslot,
                     s=(k == 0), e=(k == FG - 1):
                     E.matmul(b[:, 0:TCH], w[:, k * 128:(k + 1) * 128],
                              htblk(k), start=s, stop=e),
                     st if k == FG - 1 else None)
            c._p3mm2[dj] = st.n
            p.wait("vector", st, c._p3mm2[dj])
            p.op("vector", lambda E, dj=dj, b=bank:
                 E.tensor_add(XU[dj], XU[dj], b[:, 0:TCH]), sv)
            evac_x2[dj] = sv.n
        x2_v = sv.n

        if tap_x2 is not None:
            p.wait("sync", sv, x2_v)
            p.dma("sync", qs, tap_x2[tt, :, :], c.ARW[:, 0:DG * TCH])

        sums_v = plan_feat_sums(c, p, XU, TCH, [(sv, x2_v)])
        p.wait("vector", st, sums_v)
        p.op("vector", lambda E: E.tensor_scalar_mul(R1, c.PS_R1[0:1, 0:TCH], 1.0 / DM))
        p.op("vector", lambda E: E.tensor_scalar_mul(R2, c.PS_R2[0:1, 0:TCH], 1.0 / DM))
        p.op("vector", lambda E: E.tensor_mul(R3, R1, R1))
        p.op("vector", lambda E: E.tensor_sub(R2, R2, R3), sv)
        p.wait("scalar", sv)
        p.op("scalar", lambda E: E.activation(R2, R2, AF.Sqrt, bias=c.EPSL), sa)
        p.wait("vector", sa)
        p.op("vector", lambda E: E.reciprocal(R2, R2))
        p.op("vector", lambda E: E.tensor_mul(R3, R1, R2), sv)
        bc_v = plan_bcast_rows(c, p, [(R2, SC_BC, TCH), (R3, MSC_BC, TCH)],
                               sv, sv.n)
        p.wait("vector", sa, bc_v)
        for g in range(DG):
            p.op("vector", lambda E, g=g: E.tensor_mul(LN2[g], XU[g], SC_BC))
            p.op("vector", lambda E, g=g: E.tensor_sub(LN2[g], LN2[g], MSC_BC),
                 sv if g == DG - 1 else None)
        ln2_v = sv.n

        if tap_ln2 is not None:
            p.wait("sync", sv, ln2_v)
            p.dma("sync", qs, tap_ln2[tt, :, :], c.SMA[:, 0:DG * TCH])

        p.wait("tensor", sv, ln2_v)
        lg_evac = 0
        tb_evac = [0, 0]
        for sti in range(4):
            col = sti * 128
            p.wait("scalar", qs, qs.n)
            for src, stg in ((XU, TMS2), (LN2, TMS1)):
                for half in range(2):
                    bank = c.PS[4] if half == 0 else c.PS[5]
                    p.wait("tensor", sa, tb_evac[half])
                    for q in range(4):
                        g = half * 4 + q
                        p.op("tensor", lambda E, g=g, q=q, b=bank, src=src, col=col:
                             E.transpose(b[:, q * 128:(q + 1) * 128],
                                         src[g][:, col:col + 128], c.IDENT),
                             st if q == 3 else None)
                    tv = st.n
                    p.wait("scalar", st, tv)
                    p.op("scalar", lambda E, b=bank, stg=stg, half=half:
                         E.copy(stg[:, half * 512:(half + 1) * 512], b[:, 0:512]), sa)
                    tb_evac[half] = c.sa.n
            tm_v = sa.n
            p.wait("sync", sa, tm_v)
            row0 = t0 + col
            p.dma("sync", qs, c.x2tm[row0:row0 + 128, :], TMS2)
            p.dma("sync", qs, c.ag1_in[row0:row0 + 128, :], TMS1)

            p.wait("tensor", sa, lg_evac)
            for k in range(DG):
                p.op("tensor", lambda E, k=k, col=col, s=(k == 0), e=(k == DG - 1):
                     E.matmul(c.PS[2][:, 0:NEXP],
                              c.SMA[:, k * TCH + col:k * TCH + col + 128],
                              c.WRS[:, k * NEXP:(k + 1) * NEXP], start=s, stop=e),
                     st if k == DG - 1 else None)
            lg_t = st.n
            p.wait("scalar", st, lg_t)
            p.op("scalar", lambda E: E.copy(LG, c.PS[2][:, 0:NEXP]), sa)
            lg_evac = sa.n
            lgd = c.tap("lgdbg", [5, 128, 64])
            if lgd is not None and tt == 0 and sti == 0:
                p.wait("sync", sa, lg_evac)
                p.dma("sync", qs, lgd[0, :, 0:8], LG)
                p.dma("sync", qs, lgd[1, :, :], c.WRS)
            p.wait("vector", sa)
            p.op("vector", lambda E: E.reduce_max(MX, LG, AX.X), sv)
            p.wait("scalar", sv)
            p.op("scalar", lambda E: E.copy(MX8B, fbc(MX, 8)), sa)
            p.wait("vector", sa)
            p.op("vector", lambda E: E.tensor_tensor(
                SCRAP, LG, MX8B, OP.subtract), sv)
            p.wait("scalar", sv)
            p.op("scalar", lambda E: E.activation(EXPO, SCRAP, AF.Exp), sa)
            p.wait("vector", sa)
            p.op("vector", lambda E: E.reduce_sum(SSUM, EXPO, AX.X))
            p.op("vector", lambda E: E.tensor_tensor(EQD, LG, MX8B, OP.is_equal))
            p.op("vector", lambda E: E.memset(SPCV, 0.0))
            p.op("vector", lambda E: E.memset(SPCV, 1.0))
            p.op("vector", lambda E: E.reciprocal(PMX, SSUM))
            p.op("vector", lambda E: E.tensor_mul(MUL8, EQD, c.JC8))
            p.op("vector", lambda E: E.memset(SPCV, 0.0))
            p.op("vector", lambda E: E.memset(SPCV, 1.0))
            p.op("vector", lambda E: E.reduce_sum(IDX, MUL8, AX.X), sv)
            rt_v = sv.n
            p.wait("sync", sv, rt_v)
            p.dma("sync", qs, c.rt_in[0:1, row0:row0 + 128], IDX)
            p.dma("sync", qs, c.rt_in[1:2, row0:row0 + 128], PMX)
            lgd2 = c.taps.get("lgdbg")
            if lgd2 is not None and tt == 0 and sti == 0:
                p.dma("sync", qs, lgd2[2, :, 0:1], MX, slow=True)
                p.dma("sync", qs, lgd2[2, :, 1:2], SSUM, slow=True)
                p.dma("sync", qs, lgd2[2, :, 2:3], PMX, slow=True)
                p.dma("sync", qs, lgd2[2, :, 3:4], IDX, slow=True)
                p.wait("sync", qs, qs.n)
                p.dma("sync", qs, lgd2[3, 0:1, 0:64], c.rt_in[1:2, 0:64])
                p.dma("sync", qs, lgd2[3, 1:2, 0:64], c.rt_in[0:1, 0:64])

    barrier(c, p)


def fbc(col_ap, n):
    return bass.AP(col_ap.tensor, col_ap.offset, [list(col_ap.ap[0]), [0, n]])


def plan_p45(c, p):
    ql, qs, qg, st, sv, sa, sg, cc = (c.ql, c.qs, c.qg, c.st, c.sv, c.sa,
                                      c.sg, c.cc)
    barrier(c, p)
    rg = [list(range(NCORE))]

    p.op("gpsimd", lambda E: E.collective_compute(
        "AllGather", OP.bypass, replica_groups=rg,
        ins=[c.ag1_in.opt()], outs=[c.ag1_out.opt()]), cc)
    p.op("gpsimd", lambda E: E.collective_compute(
        "AllGather", OP.bypass, replica_groups=rg,
        ins=[c.rt_in.opt()], outs=[c.rt_out.opt()]), cc)
    rt_cc = cc.n
    p.wait("sync", cc, rt_cc)
    if c.p45_cut <= 1:
        barrier(c, p)
        return

    IB8 = c.ARX[32:40, 0:8192]
    ME = c.ARZ[32:40, 0:8192]
    CUM = c.ARX[32:40, 0:8192]
    O8C = c.ARW[32:40, 100:101]
    ECOL8 = c.CONST[32:40, 768:769]
    OHME8 = c.CONST[32:40, 769:770]
    CARRY = c.STAT[32:40, 1100:1101]
    c.SPC8 = c.STAT[32:40, 1132:1136]
    c.SPC8B = c.ARY[32:40, 4096:4608]
    c.SPC8C = c.ARY[32:40, 4608:5120]
    BASE8 = c.STAT[32:40, 1104:1112]
    TOT = c.STAT[32:40, 1112:1113]
    OHB = c.STAT[32:40, 1120:1128]
    BASEME = c.STAT[32:40, 1128:1129]

    p.op("vector", lambda E: E.memset(O8C, 1.0))
    p.op("vector", lambda E: E.memset(CARRY, 0.0))
    p.op("vector", lambda E: E.memset(BASE8, 0.0), sv)

    for h in range(2):
        p.wait("sync", sv, sv.n)
        p.wait("sync", sa, sa.n)
        srcap = bass.AP(c.rt_out.tensor, h * 4 * 2 * L, [[2 * L, 4], [1, L]])
        for k in range(8):
            p.dma("sync", ql, c.ARX[32 + k:33 + k, 0:8192], srcap)
        ld = ql.n
        p.wait("vector", ql, ld)
        p.op("vector", lambda E: E.memset(ME, 0.0))
        p.op("vector", lambda E: E.memset(ME, 1.0))
        p.op("vector", lambda E: E.tensor_tensor(ME, IB8, fbc(ECOL8, 8192), OP.is_equal))
        p.op("vector", lambda E: E.tensor_tensor_scan(
            CUM, fbc(O8C, 8192), ME, 0.0, OP.mult, OP.add))
        p.op("vector", lambda E: E.memset(c.SPC8B, 0.0))
        p.op("vector", lambda E: E.memset(c.SPC8B, 1.0))
        p.op("vector", lambda E: E.memset(c.SPC8B, 2.0))
        bcols = bass.AP(CUM.tensor, CUM.offset + 2047,
                        [list(CUM.ap[0]), [2048, 4]])
        if h == 0:
            p.op("vector", lambda E, bcols=bcols: E.tensor_copy(
                BASE8[:, 1:5], bcols))
            p.op("vector", lambda E: E.memset(c.SPC8B, 3.0))
            p.op("vector", lambda E: E.memset(c.SPC8B, 4.0))
            p.op("vector", lambda E: E.tensor_copy(CARRY, BASE8[:, 4:5]), sv)
        else:
            p.op("vector", lambda E, bcols=bcols: E.tensor_copy(
                c.STAT[32:40, 1140:1144], bcols))
            p.op("vector", lambda E: E.memset(c.SPC8B, 3.0))
            p.op("vector", lambda E: E.memset(c.SPC8B, 4.0))
            p.op("vector", lambda E: E.tensor_tensor(
                c.STAT[32:40, 1140:1144], c.STAT[32:40, 1140:1144],
                fbc(CARRY, 4), OP.add))
            p.op("vector", lambda E: E.memset(c.SPC8B, 5.0))
            p.op("vector", lambda E: E.memset(c.SPC8B, 6.0))
            p.op("vector", lambda E: E.tensor_copy(
                BASE8[:, 5:8], c.STAT[32:40, 1140:1143]))
            p.op("vector", lambda E: E.tensor_copy(
                TOT, c.STAT[32:40, 1143:1144]), sv)
        scan_v = sv.n
    base_v = sv.n
    if c.p45_cut <= 2:
        barrier(c, p)
        return

    for k in range(8):
        p.dma("sync", ql, c.STAT[32 + k:33 + k, 1120:1128], c.ohprev[:, :])
    ohb_ld = ql.n
    p.wait("vector", ql, ohb_ld)
    p.wait("vector", sa, sa.n)
    p.op("vector", lambda E: E.tensor_mul(OHB, OHB, BASE8))
    p.op("vector", lambda E: E.memset(c.SPC8B, 0.0))
    p.op("vector", lambda E: E.memset(c.SPC8B, 1.0))
    p.op("vector", lambda E: E.reduce_sum(BASEME, OHB, AX.X), sv)
    if c.p45_cut <= 21:
        barrier(c, p)
        return

    IDXL = c.ARX[0:1, 0:2048]
    PMXL = c.ARX[0:1, 2048:4096]
    RNK = c.ARX[0:1, 4096:6144]
    BTOK = c.ARX[0:1, 6144:8192]
    GRANK = c.ARZ[0:1, 0:2048]
    KEPT = c.ARZ[0:1, 2048:4096]
    POSC = c.ARZ[0:1, 4096:6144]
    FLAT = c.ARZ[0:1, 6144:8192]
    AFAC = c.ARW[0:1, 0:2048]
    BFAC = c.ARW[0:1, 2048:4096]
    TLOCR = c.SMA[0:1, 0:2048]
    SIDXR = c.SMA[0:1, 2048:4096]
    KF = c.ARY[0:1, 2048:4096]
    MLOC8 = c.ARY[32:40, 0:2048]
    CUML = c.ARW[32:40, 2048:4096]
    SCR8 = c.ARY[32:40, 2048:4096]

    p.wait("sync", sv, sv.n)
    p.dma("sync", ql, IDXL, c.rt_in[0:1, :])
    p.dma("sync", ql, PMXL, c.rt_in[1:2, :])
    for k in range(8):
        p.dma("sync", ql, c.ARW[32 + k:33 + k, 2048:4096], c.rt_in[0:1, :])
    ldl = ql.n
    dbg = c.tap("p45dbg", [8, 2048])
    if dbg is not None:
        p.wait("sync", ql, ldl)
        p.dma("sync", qs, dbg[0, :], IDXL)
        p.dma("sync", qs, dbg[1, :], PMXL)
    p.wait("vector", ql, ldl)
    p.op("vector", lambda E: E.memset(MLOC8, 0.0))
    p.op("vector", lambda E: E.memset(MLOC8, 1.0))
    p.op("vector", lambda E: E.tensor_tensor(MLOC8, CUML, fbc(ECOL8, 2048), OP.is_equal))
    p.op("vector", lambda E: E.tensor_tensor_scan(
        CUML, fbc(O8C, 2048), MLOC8, 0.0, OP.mult, OP.add))
    p.op("vector", lambda E: E.tensor_scalar_add(CUML, CUML, -1.0))
    p.op("vector", lambda E: E.tensor_mul(CUML, CUML, MLOC8))
    p.op("vector", lambda E: E.tensor_tensor(SCR8, MLOC8, fbc(BASEME, 2048), OP.mult), sv)
    sc8_v = sv.n
    if c.p45_cut <= 22:
        barrier(c, p)
        return
    p.wait("tensor", sv, sc8_v)
    for j, (src, dst) in enumerate(((CUML, RNK), (SCR8, BTOK))):
        for q in range(4):
            if j or q:
                p.wait("tensor", sa, sa.n)
            p.op("tensor", lambda E, src=src, q=q: E.matmul(
                c.PS_R1[0:1, 0:512], O8C, src[:, q * 512:(q + 1) * 512],
                start=True, stop=True), st)
            p.wait("scalar", st)
            p.op("scalar", lambda E, dst=dst, q=q:
                 E.copy(dst[0:1, q * 512:(q + 1) * 512], c.PS_R1[0:1, 0:512]), sa)
    red_v = sa.n
    if c.p45_cut <= 23:
        barrier(c, p)
        return

    p.wait("vector", sa, red_v)
    p.op("vector", lambda E: E.tensor_add(GRANK, RNK, BTOK))
    p.op("vector", lambda E: E.tensor_single_scalar(KEPT, GRANK, 2560.0, OP.is_lt))
    p.op("vector", lambda E: E.tensor_scalar_min(POSC, GRANK, 2559.0))
    p.op("vector", lambda E: E.tensor_scalar_mul(FLAT, IDXL, 2560.0))
    p.op("vector", lambda E: E.tensor_add(FLAT, FLAT, POSC))
    p.op("vector", lambda E: E.tensor_mul(AFAC, KEPT, PMXL))
    p.op("vector", lambda E: E.tensor_scalar(BFAC, KEPT, -1.0, 1.0, OP.mult, OP.add), sv)
    fac_v = sv.n
    if dbg is not None:
        p.wait("sync", sv, fac_v)
        p.dma("sync", qs, dbg[2, :], GRANK)
        p.dma("sync", qs, dbg[6, 8:16], CARRY, slow=True)
        p.dma("sync", qs, dbg[6, 16:24], TOT, slow=True)
        p.dma("sync", qs, dbg[6, 24:32], BASEME, slow=True)
        p.dma("sync", qs, dbg[3, :], FLAT)
        p.dma("sync", qs, dbg[4, :], AFAC)
        p.dma("sync", qs, dbg[5, 0:64], c.STAT[32:40, 1104:1112])
        p.dma("sync", qs, dbg[6, 0:8], BASEME)
    if c.p45_cut <= 24:
        barrier(c, p)
        return
    p.wait("sync", sv, fac_v)
    p.dma("sync", qs, c.fac_d[0:1, :], AFAC)
    p.dma("sync", qs, c.fac_d[1:2, :], BFAC)
    c.fac_st = qs.n
    if c.p45_cut <= 25:
        barrier(c, p)
        return
    p.op("vector", lambda E: E.tensor_copy(c.I32B[0:1, 0:2048], FLAT), sv)
    flat_v = sv.n
    p.wait("sync", sv, flat_v)
    p.dma("sync", qs, c.flat_d[:, :], c.I32B[0:1, 0:2048])
    c.flat_st = qs.n
    if c.p45_cut <= 30:
        barrier(c, p)
        return

    ZB = c.ARW[:, 4300:4476]
    p.op("vector", lambda E: E.memset(ZB, 0.0), sv)
    zb_v = sv.n
    p.wait("sync", sv, zb_v)
    p.dma("sync", qs, c.tab_in[0:1, :], ZB)
    tab_z = qs.n

    p.dma("sync", ql, TLOCR, c.tloc[:, :])
    tl_ld = ql.n
    p.wait("vector", ql, tl_ld)
    p.wait("vector", qs, c.flat_st)
    p.op("vector", lambda E: E.tensor_scalar_add(SIDXR, TLOCR, 20480.0))
    p.op("vector", lambda E: E.tensor_mul(SIDXR, SIDXR, BFAC))
    p.op("vector", lambda E: E.tensor_mul(KF, KEPT, FLAT))
    p.op("vector", lambda E: E.tensor_add(SIDXR, SIDXR, KF))
    p.op("vector", lambda E: E.tensor_copy(c.I32B[0:1, 0:2048], SIDXR), sv)
    sidx_v = sv.n
    if dbg is not None:
        p.wait("sync", sv, sidx_v)
        p.dma("sync", qs, dbg[7, :], SIDXR)
    p.wait("sync", sv, sidx_v)
    p.dma("sync", qs, c.sidx_d[:, :], c.I32B[0:1, 0:2048])
    sidx_st = qs.n
    p.wait("sync", qs, sidx_st)
    TGT = c.STAT[:, 0:16]
    for i in range(16):
        p.dma("sync", ql, c.I32T[:, i:i + 1], c.sidx_d[0:1, i * 128:(i + 1) * 128])
        p.dma("sync", ql, TGT[:, i:i + 1], c.tgid1[0:1, i * 128:(i + 1) * 128])
    sct_ld = ql.n
    if c.p45_cut <= 40:
        barrier(c, p)
        return
    p.wait("gpsimd", ql, sct_ld)
    p.wait("gpsimd", qs, tab_z)
    for i in range(16):
        ii = c.I32T[:, i:i + 1]
        vv = TGT[:, i:i + 1]
        qg.n += 16
        p.emit("gpsimd", lambda E, h=qg.h, ii=ii, vv=vv: E.indirect_dma_start(
            c.tab_in[0:1, :], bass.IndirectOffsetOnAxis(ii, 1),
            vv, None).then_inc(h, 16))
    sct_v = qg.n
    if c.p45_cut <= 50:
        barrier(c, p)
        return
    p.wait("gpsimd", qg, sct_v)
    p.op("gpsimd", lambda E: E.collective_compute(
        "AllReduce", OP.add, replica_groups=rg,
        ins=[c.tab_in.opt()], outs=[c.tab_red.opt()]), cc)
    tab_cc = cc.n
    if c.p45_cut <= 60:
        barrier(c, p)
        return

    MRG = c.ARY[32:40, 0:CAP]
    GIDR = c.ARX[0:1, 0:CAP]
    p.wait("sync", cc, tab_cc)
    p.wait("sync", sv, sv.n)
    p.wait("sync", sa, sa.n)
    mrgap = bass.AP(c.tab_red.tensor, 0, [[CAP, 8], [1, CAP]])
    p.dma("sync", ql, MRG, mrgap)
    p.wait("vector", ql, ql.n)
    p.op("vector", lambda E: E.memset(c.SPC8B, 0.0))
    p.op("vector", lambda E: E.memset(c.SPC8B, 1.0))
    p.op("vector", lambda E: E.memset(c.SPC8B, 2.0))
    p.op("vector", lambda E: E.tensor_tensor(MRG, MRG, fbc(OHME8, CAP), OP.mult), sv)
    p.wait("tensor", sv, sv.n)
    for q in range(ETT):
        if q:
            p.wait("tensor", sa, sa.n)
        p.op("tensor", lambda E, q=q: E.matmul(
            c.PS_R1[0:1, 0:512], O8C, MRG[:, q * 512:(q + 1) * 512],
            start=True, stop=True), st)
        p.wait("scalar", st)
        p.op("scalar", lambda E, q=q:
             E.copy(GIDR[0:1, q * 512:(q + 1) * 512], c.PS_R1[0:1, 0:512]), sa)
    p.wait("vector", sa, sa.n)
    p.op("vector", lambda E: E.tensor_scalar_add(GIDR, GIDR, -1.0))
    p.op("vector", lambda E: E.tensor_scalar_max(GIDR, GIDR, 0.0))
    p.op("vector", lambda E: E.tensor_copy(c.I32B[0:1, 0:CAP], GIDR), sv)
    p.wait("sync", sv, sv.n)
    p.dma("sync", qs, c.gidx_d[:, :], c.I32B[0:1, 0:CAP])
    c.gidx_st = qs.n
    if c.p45_cut <= 70:
        barrier(c, p)
        return

    NM = c.STAT[32:40, 1130:1131]
    NSC = c.STAT[0:1, 1136:1137]
    NROW = c.ARX[0:1, 4096:4104]
    p.op("vector", lambda E: E.tensor_mul(NM, TOT, OHME8), sv)
    p.wait("tensor", sv, sv.n)
    p.op("tensor", lambda E: E.matmul(c.PS_R2[0:1, 0:1], O8C, NM,
                                      start=True, stop=True), st)
    p.wait("scalar", st, st.n)
    p.op("scalar", lambda E: E.copy(NSC, c.PS_R2[0:1, 0:1]))
    p.op("scalar", lambda E: E.copy(c.SPC8C, c.SPC8B))
    p.op("scalar", lambda E: E.copy(c.SPC8C, c.SPC8B), sa)
    p.wait("vector", sa, sa.n)
    p.op("vector", lambda E: E.tensor_scalar_min(NSC, NSC, 2560.0))
    p.op("vector", lambda E: E.tensor_scalar_mul(NROW, c.OFFS, -1.0))
    p.op("vector", lambda E: E.memset(c.SPC8B, 0.0))
    p.op("vector", lambda E: E.memset(c.SPC8B, 1.0))
    p.op("vector", lambda E: E.tensor_tensor(NROW, NROW, fbc(NSC, 8), OP.add))
    p.op("vector", lambda E: E.memset(c.SPC8B, 0.0))
    p.op("vector", lambda E: E.memset(c.SPC8B, 1.0))
    p.op("vector", lambda E: E.tensor_scalar_max(NROW, NROW, 0.0))
    p.op("vector", lambda E: E.memset(c.SPC8B, 0.0))
    p.op("vector", lambda E: E.memset(c.SPC8B, 1.0))
    p.op("vector", lambda E: E.tensor_scalar_min(NROW, NROW, 512.0), sv)
    p.wait("sync", sv, sv.n)
    p.dma("sync", qs, c.nrow_d[:, :], NROW)
    tap_rt = c.tap("route", [6, 2600])
    if tap_rt is not None:
        p.wait("sync", sv, sv.n)
        p.wait("sync", sa, sa.n)
        p.dma("sync", qs, tap_rt[0, 0:2048], FLAT)
        p.dma("sync", qs, tap_rt[1, 0:2048], AFAC)
        p.dma("sync", qs, tap_rt[2, 0:2048], BFAC)
        p.dma("sync", qs, tap_rt[3, 0:CAP], GIDR)
        p.dma("sync", qs, tap_rt[4, 0:8], NROW)
        p.dma("sync", qs, tap_rt[5, 0:2048], GRANK)
    barrier(c, p)


def plan_p6(c, p):
    ql, qs, qg, st, sv, sa, sg, cc = (c.ql, c.qs, c.qg, c.st, c.sv, c.sa,
                                      c.sg, c.cc)
    barrier(c, p)
    rg = [list(range(NCORE))]

    EINW = c.ARX[:, 0:4096]
    def eint(tt, g):
        base = (c.ARY, c.ARY, c.ARZ, c.ARZ, c.SMA)[tt]
        off = (0, 4096, 0, 4096, 0)[tt]
        return base[:, off + g * 512:off + (g + 1) * 512]

    p.wait("sync", qs, c.gidx_st)
    for t in range(20):
        p.dma("sync", ql, c.I32T[:, t:t + 1],
              c.gidx_d[0:1, t * 128:(t + 1) * 128])
    gid_ld = ql.n

    ag1_cc_done = 2
    p.wait("gpsimd", cc, ag1_cc_done)
    p.wait("gpsimd", ql, gid_ld)
    c._p6ev = [0] * DG
    tr_done = 0
    for r in range(ETT):
        p.wait("gpsimd", st, tr_done)
        for b in range(4):
            t = r * 4 + b
            dst = c.ARX[:, b * 1024:(b + 1) * 1024]
            idx = c.I32T[:, t:t + 1]
            qg.n += 16
            p.emit("gpsimd", lambda E, h=qg.h, dst=dst, idx=idx:
                   E.indirect_dma_start(dst, None, c.ag1_out[:, :],
                   bass.IndirectOffsetOnAxis(idx, 0)).then_inc(h, 16))
        gat_v = qg.n
        p.wait("tensor", qg, gat_v)
        for g in range(DG):
            bank = c.PS[4] if g % 2 == 0 else c.PS[5]
            p.wait("tensor", sa, c._p6ev[g - 2] if g >= 2 else c._p6ev[g + 6])
            for b in range(4):
                p.op("tensor", lambda E, g=g, b=b, bank=bank: E.transpose(
                    bank[:, b * 128:(b + 1) * 128],
                    EINW[:, b * 1024 + g * 128:b * 1024 + (g + 1) * 128],
                    c.IDENT), st if b == 3 else None)
            tv = st.n
            p.wait("scalar", st, tv)
            p.op("scalar", lambda E, r=r, g=g, bank=bank:
                 E.copy(eint(r, g), bank[:, 0:512]), sa)
            c._p6ev[g] = sa.n
        tr_done = st.n
    tap_ein = c.tap("ein", [ETT, 128, DG * 512])
    if tap_ein is not None:
        barrier(c, p)
        for r in range(ETT):
            for g in range(DG):
                p.dma("sync", qs, tap_ein[r, :, g * 512:(g + 1) * 512], eint(r, g))
    barrier(c, p)

    HSTG = [c.STAT[:, 0:512], c.STAT[:, 512:1024]]
    slot = [c.WDBL[:, 0:1024], c.WDBL[:, 1024:2048]]
    mm_done = [0] * FG
    stg_free = [0, 0]
    evac_done = [0, 0]
    for fj in range(FG):
        if fj >= 2:
            p.wait("sync", st, mm_done[fj - 2])
        p.dma("sync", ql, slot[fj % 2], c.w_e1[fj, :, :])
        ld = ql.n
        p.wait("tensor", ql, ld)
        for tt in range(ETT):
            q = fj * ETT + tt
            bank = c.PS[q % 2]
            sidx = q % 2
            p.wait("tensor", sa, evac_done[q % 2])
            for k in range(DG):
                p.op("tensor", lambda E, k=k, b=bank, sl=slot[fj % 2], tt=tt,
                     s=(k == 0), e=(k == DG - 1):
                     E.matmul(b[:, 0:TCH], sl[:, k * 128:(k + 1) * 128],
                              eint(tt, k), start=s, stop=e),
                     st if k == DG - 1 else None)
            mmv = st.n
            p.wait("scalar", st, mmv)
            p.wait("scalar", c.qs, stg_free[sidx])
            p.op("scalar", lambda E, b=bank, sidx=sidx:
                 E.activation(HSTG[sidx], b[:, 0:TCH], AF.Relu), sa)
            hv = sa.n
            evac_done[q % 2] = hv
            p.wait("sync", sa, hv)
            p.dma("sync", qs, c.he_fm[fj * 128:(fj + 1) * 128,
                                      tt * TCH:(tt + 1) * TCH], HSTG[sidx])
            stg_free[sidx] = qs.n
        mm_done[fj] = st.n
    barrier(c, p)

    def htblk(k):
        ar = c.ARX if k < 16 else c.ARY
        return ar[:, (k % 16) * 512:(k % 16) * 512 + 512]

    EOUT = [c.SMA[:, g * 512:(g + 1) * 512] for g in range(DG)]
    W2S = [c.ARZ[:, 0:4096], c.ARZ[:, 4096:8192]]
    TMS = c.STAT[:, 0:1024]
    w2mm = [0] * DG
    for tt in range(ETT):
        t0 = tt * TCH
        barrier(c, p)
        for k in range(FG):
            p.dma("sync", ql, htblk(k), c.he_fm[k * 128:(k + 1) * 128, t0:t0 + TCH])
        ld_v = ql.n
        p.wait("tensor", ql, ld_v)
        evac_eo = [0] * DG
        for dj in range(DG):
            wslot = W2S[dj % 2]
            if dj >= 2:
                p.wait("sync", st, w2mm[dj - 2])
            p.dma("sync", ql, wslot, c.w_e2[dj, :, :])
            wld = ql.n
            p.wait("tensor", ql, wld)
            bank = c.PS[0] if dj % 2 == 0 else c.PS[1]
            if dj >= 2:
                p.wait("tensor", sa, evac_eo[dj - 2])
            for k in range(FG):
                p.op("tensor", lambda E, k=k, b=bank, w=wslot,
                     s=(k == 0), e=(k == FG - 1):
                     E.matmul(b[:, 0:TCH], w[:, k * 128:(k + 1) * 128],
                              htblk(k), start=s, stop=e),
                     st if k == FG - 1 else None)
            w2mm[dj] = st.n
            p.wait("scalar", st, w2mm[dj])
            p.op("scalar", lambda E, dj=dj, b=bank: E.copy(EOUT[dj], b[:, 0:TCH]), sa)
            evac_eo[dj] = sa.n
        eo_v = sa.n
        p.wait("tensor", sa, eo_v)
        tb_evac = [0, 0]
        for sti in range(4):
            col = sti * 128
            p.wait("scalar", qs, qs.n)
            for half in range(2):
                bank = c.PS[4] if half == 0 else c.PS[5]
                p.wait("tensor", sa, tb_evac[half])
                for qq in range(4):
                    g = half * 4 + qq
                    p.op("tensor", lambda E, g=g, qq=qq, b=bank, col=col:
                         E.transpose(b[:, qq * 128:(qq + 1) * 128],
                                     EOUT[g][:, col:col + 128], c.IDENT),
                         st if qq == 3 else None)
                tv = st.n
                p.wait("scalar", st, tv)
                p.op("scalar", lambda E, b=bank, half=half:
                     E.copy(TMS[:, half * 512:(half + 1) * 512], b[:, 0:512]), sa)
                tb_evac[half] = sa.n
            p.wait("sync", sa, sa.n)
            p.dma("sync", qs, c.ag2_in[t0 + col:t0 + col + 128, :], TMS)
            tap_eo = c.tap("eo", [CAP, DM]) if (tt == 0 and sti == 0) else c.taps.get("eo")
            if tap_eo is not None:
                p.dma("sync", qs, tap_eo[t0 + col:t0 + col + 128, :], TMS)
    barrier(c, p)
    p.op("gpsimd", lambda E: E.collective_compute(
        "AllGather", OP.bypass, replica_groups=rg,
        ins=[c.ag2_in.opt()], outs=[c.ag2_out.opt()]), cc)
    c.ag2_cc = cc.n


def plan_p7(c, p):
    ql, qs, qg, st, sv, sa, sg, cc = (c.ql, c.qs, c.qg, c.st, c.sv, c.sa,
                                      c.sg, c.cc)
    barrier(c, p)
    p.wait("gpsimd", cc, c.ag2_cc)
    p.wait("sync", cc, c.ag2_cc)

    p.wait("sync", qs, c.flat_st)
    for i in range(16):
        p.dma("sync", ql, c.I32T[:, 20 + i:21 + i],
              c.flat_d[0:1, i * 128:(i + 1) * 128])
    fl_ld = ql.n
    AF16 = c.STAT[:, 1200:1216]
    BF16c = c.STAT[:, 1216:1232]
    p.dma("sync", ql, AF16,
          bass.AP(c.fac_d.tensor, 0, [[1, 128], [128, 16]]), slow=True)
    p.dma("sync", ql, BF16c,
          bass.AP(c.fac_d.tensor, L, [[1, 128], [128, 16]]), slow=True)
    fac_ld = ql.n

    def gatm(i):
        ar = c.ARX if i < 8 else c.ARY
        return ar[:, (i % 8) * 1024:(i % 8) * 1024 + 1024]

    T1 = c.SMA[:, 0:1024]
    T2 = c.SMA[:, 1024:2048]
    OUTT = c.SMA[:, 2048:3072]
    ld_free = [0] * 4
    out_stored = [0] * 4
    comb_done = [0] * 16
    p.wait("gpsimd", ql, fl_ld)
    for i in range(16):
        if i >= 8:
            p.wait("gpsimd", sv, comb_done[i - 8])
        idx = c.I32T[:, 20 + i:21 + i]
        dst = gatm(i)
        qg.n += 16
        p.emit("gpsimd", lambda E, h=qg.h, dst=dst, idx=idx:
               E.indirect_dma_start(dst, None, c.ag2_out[:, :],
               bass.IndirectOffsetOnAxis(idx, 0)).then_inc(h, 16))
        gat_i = qg.n

        sl = i % 4
        X2B = c.ARZ[:, sl * 1024:(sl + 1) * 1024]
        LNB = c.ARZ[:, 4096 + sl * 1024:4096 + (sl + 1) * 1024]
        p.wait("sync", sv, ld_free[sl])
        p.dma("sync", ql, X2B, c.x2tm[i * 128:(i + 1) * 128, :])
        p.dma("sync", ql, LNB, c.ag1_in[i * 128:(i + 1) * 128, :])
        bl_ld = ql.n
        p.wait("vector", ql, bl_ld)
        p.wait("vector", ql, fac_ld)
        p.wait("vector", qg, gat_i)
        p.wait("vector", qs, out_stored[(i + 1) % 4] if i >= 3 else 0)
        p.op("vector", lambda E, i=i, LNB=LNB: E.tensor_tensor(
            T1, LNB, fbc(BF16c[:, i:i + 1], 1024), OP.mult))
        p.op("vector", lambda E, i=i: E.tensor_tensor(
            T2, gatm(i), fbc(AF16[:, i:i + 1], 1024), OP.mult))
        p.op("vector", lambda E: E.tensor_add(T2, T2, T1))
        p.op("vector", lambda E, i=i, X2B=X2B: E.tensor_add(OUTT, X2B, T2), sv)
        ov = sv.n
        comb_done[i] = ov
        ld_free[sl] = ov
        p.wait("sync", sv, ov)
        p.dma("sync", qs, c.y_out[i * 128:(i + 1) * 128, :], OUTT)
        out_stored[i % 4] = qs.n


def plan_finalize(c, p):
    p.wait("gpsimd", c.qs)
    p.wait("gpsimd", c.qg)


def _tile_w(wt, kdim, mdim):
    kk, mm = kdim // 128, mdim // 128
    return np.ascontiguousarray(
        wt.reshape(kk, 128, mm, 128).transpose(2, 1, 0, 3).reshape(mm, 128, kk * 128))


def _fm_pack(v, ngrp, inner):
    return np.ascontiguousarray(
        v.reshape(ngrp, 128, inner).transpose(1, 0, 2).reshape(128, ngrp * inner))


def make_inputs(core, I):
    x = np.asarray(I["L0_attn_x"])
    ipw = np.asarray(I["L0_attn_in_proj_w"])
    xpw = np.asarray(I["L0_attn_x_proj_w"])
    dpw = np.asarray(I["L0_attn_dt_proj_w"])
    dpb = np.asarray(I["L0_attn_dt_proj_b"])
    cw = np.asarray(I["L0_attn_conv_w"])
    cb = np.asarray(I["L0_attn_conv_b"])
    alog = np.asarray(I["L0_attn_A_log"])
    dd = np.asarray(I["L0_attn_D"])
    opw = np.asarray(I["L0_attn_out_proj_w"])
    m1 = np.asarray(I["L0_attn_mlp_w1"])
    m2 = np.asarray(I["L0_attn_mlp_w2"])
    wr = np.asarray(I["L0_ffn_Wr"])
    w1 = np.asarray(I["L0_ffn_W1"])
    w2 = np.asarray(I["L0_ffn_W2"])

    im = {
        "x_fm": x[core].T,
        "w_inproj": _tile_w(ipw.T, DM, 2 * DIN),
        "w_xproj": _fm_pack(xpw.T, NG, 96),
        "w_dtproj": dpw.T,
        "dtb_fm": _fm_pack(dpb.reshape(DIN, 1), NG, 1),
        "convw_fm": _fm_pack(cw, NG, DCONV),
        "convb_fm": _fm_pack(cb.reshape(DIN, 1), NG, 1),
        "alog_fm": _fm_pack(alog, NG, DS),
        "d_fm": _fm_pack(dd.reshape(DIN, 1), NG, 1),
        "w_oproj": _tile_w(opw.T, DIN, DM),
        "w_mlp1": _tile_w(m1.T, DM, DFF),
        "w_mlp2": _tile_w(m2.T, DFF, DM),
        "w_r": _fm_pack(wr, DG, NEXP),
        "w_e1": _tile_w(w1[core], DM, DFF),
        "w_e2": _tile_w(w2[core], DFF, DM),
        "ident": np.eye(128, dtype=np.float32),
        "ecol": (np.arange(128, dtype=np.float32) % 16).reshape(128, 1),
        "ohprev": (np.arange(NEXP) == core).astype(np.float32).reshape(1, NEXP),
        "ohme": np.tile((np.arange(NEXP) == core).astype(np.float32), 16).reshape(128, 1),
        "tloc": np.arange(L, dtype=np.float32).reshape(1, L),
        "offs": (np.arange(8, dtype=np.float32) * 512).reshape(1, 8),
        "jcol": np.tile(np.arange(8, dtype=np.float32), 128).reshape(128, 8),
        "tgid1": (core * L + np.arange(L) + 1).astype(np.float32).reshape(1, L),
    }
    return {k: np.ascontiguousarray(v, dtype=np.float32) for k, v in im.items()}


def kernel(**inputs):
    nc, c = build_program(stage=5)
    in_maps = [make_inputs(core, inputs) for core in range(NCORE)]
    res = run_bass_kernel_spmd(nc, in_maps, core_ids=list(range(NCORE)))
    out = np.stack([np.asarray(r["y"]) for r in res.results], axis=0)
    return out.astype(np.float32)
